# version 04
# baseline (speedup 1.0000x reference)
import numpy as np
import concourse.bass as bass
import concourse.mybir as mybir
from concourse.bass_utils import run_bass_kernel_spmd
from contextlib import ExitStack

F32 = mybir.dt.float32
I16 = mybir.dt.int16
U16 = mybir.dt.uint16
AX = mybir.AxisListType.X
OP = mybir.AluOpType
AF = mybir.ActivationFunctionType
MAGIC = 12582912.0

B, S, D = 2, 2048, 1024
T = B * S
DF = 3072
NC = 8
ENGS = ("sync", "vector", "scalar", "tensor", "gpsimd")


class Plan:
    def __init__(self, sems):
        self.sems = sems
        self.cnt = {k: 0 for k in sems}
        self.ops = {e: [] for e in ENGS}
        self._fsnap = None
        self._fenced = set()

    def v(self, s):
        return self.cnt[s]

    def add(self, eng, emit, waits=(), incs=()):
        w = [(s, x) for s, x in waits if x > 0]
        if self._fsnap is not None and eng not in self._fenced:
            w = [(s, x) for s, x in self._fsnap.items() if x > 0] + w
            self._fenced.add(eng)
        self.ops[eng].append((emit, w, list(incs)))
        for s, k in incs:
            self.cnt[s] += k

    def fence(self):
        self._fsnap = dict(self.cnt)
        self._fenced = set()

    def emit(self, blk):
        own = {"vector": "sVE", "scalar": "sSC"}
        for eng in ENGS:
            ops = self.ops[eng]
            if not ops:
                continue

            def body(e, ops=ops, sems=self.sems, eng=eng):
                last = {}
                own_sem = own.get(eng)
                self_cnt = 0
                for emit, waits, incs in ops:
                    if own_sem is not None and self_cnt > 0:
                        if last.get(own_sem, 0) < self_cnt:
                            e.wait_ge(sems[own_sem], self_cnt)
                            last[own_sem] = self_cnt
                    for s, x in waits:
                        if last.get(s, 0) < x:
                            e.wait_ge(sems[s], x)
                            last[s] = x
                    ins = emit(e) if emit is not None else None
                    for s, k in incs:
                        ins.then_inc(sems[s], k)
                        if s == own_sem:
                            self_cnt += k
            getattr(blk, eng)(body)


def build_nc():
    nc = bass.Bass("TRN2", target_bir_lowering=False, debug=False,
                   detect_race_conditions=False)
    x_h = nc.dram_tensor("x", [B, S, D], F32, kind="ExternalInput").ap()
    y_h = nc.dram_tensor("y", [B, S, D], F32, kind="ExternalOutput").ap()
    w_h = {n: nc.dram_tensor(n, [D, D], F32, kind="ExternalInput").ap()
           for n in ("wq", "wk", "wv", "wo")}
    wg_h = nc.dram_tensor("wg", [DF, D], F32, kind="ExternalInput").ap()
    wu_h = nc.dram_tensor("wu", [DF, D], F32, kind="ExternalInput").ap()
    wd_h = nc.dram_tensor("wd", [D, DF], F32, kind="ExternalInput").ap()
    id_h = nc.dram_tensor("ident", [128, 128], F32, kind="ExternalInput").ap()
    tr_h = nc.dram_tensor("triu", [128, 128], F32, kind="ExternalInput").ap()

    qT_h = nc.dram_tensor("q_sc", [8, 128, S], F32, kind="Internal").ap()
    v_sc = nc.dram_tensor("v_sc", [8, 16, 128, 130], F32, kind="Internal").ap()
    x1_h = nc.dram_tensor("x1_sc", [T, D], F32, kind="Internal").ap()
    ln2_h = nc.dram_tensor("ln2_sc", [8, 128, T], F32, kind="Internal").ap()
    wT_h = {n: nc.dram_tensor(n + "t_sc", [8, 128, D], F32, kind="Internal").ap()
            for n in ("wq", "wk", "wv", "wo")}
    wgT_h = nc.dram_tensor("wgt_sc", [24, 8, 128, 128], F32, kind="Internal").ap()
    wuT_h = nc.dram_tensor("wut_sc", [24, 8, 128, 128], F32, kind="Internal").ap()
    wdT_h = nc.dram_tensor("wdt_sc", [24, 128, D], F32, kind="Internal").ap()
    graw_h = nc.dram_tensor("graw_sc", [DF, T], F32, kind="Internal").ap()
    uraw_h = nc.dram_tensor("uraw_sc", [DF, T], F32, kind="Internal").ap()
    gi_h = nc.dram_tensor("gi_sc", [DF, T], I16, kind="Internal").ap()
    mi_h = nc.dram_tensor("mi_sc", [DF, T], U16, kind="Internal").ap()
    ui_h = nc.dram_tensor("ui_sc", [DF, T], I16, kind="Internal").ap()
    gi2_h = nc.dram_tensor("gi2_sc", [DF, T], I16, kind="Internal").ap()
    ccin = [nc.dram_tensor(f"ccin{i}", [1, 8], F32, kind="Internal") for i in range(5)]
    ccout = [nc.dram_tensor(f"ccout{i}", [1, 8], F32, kind="Internal") for i in range(5)]

    st = ExitStack()
    cb = st.enter_context(nc.sbuf_tensor("cbank", [128, 4], F32))
    sc = st.enter_context(nc.sbuf_tensor("scales", [128, 12], F32))
    stt = st.enter_context(nc.sbuf_tensor("stats", [128, 8], F32))
    ident = st.enter_context(nc.sbuf_tensor("ident_s", [128, 128], F32))
    triu = st.enter_context(nc.sbuf_tensor("triu_s", [128, 128], F32))
    ones = st.enter_context(nc.sbuf_tensor("ones_s", [1, 128], F32))
    rw = st.enter_context(nc.sbuf_tensor("rw", [1, 1040], F32))
    ccs = st.enter_context(nc.sbuf_tensor("ccs", [1, 8], F32))
    ccb = st.enter_context(nc.sbuf_tensor("ccb", [1, 8], F32))
    pcol = st.enter_context(nc.sbuf_tensor("pcol", [1, 128], F32))
    sqscr = st.enter_context(nc.sbuf_tensor("sqscr", [128, 1024], F32))
    pb = [st.enter_context(nc.psum_tensor(f"pb{i}", [128, 512], F32)) for i in range(8)]
    sems = {n: st.enter_context(nc.semaphore(n))
            for n in ("sLD", "sSP", "sMM", "sVE", "sSC", "sCC")}

    P = Plan(sems)
    LD, SP, MM, VE, SCm, CC = "sLD", "sSP", "sMM", "sVE", "sSC", "sCC"
    bank_free = {}

    def dma(eng, out, in_, waits=()):
        s = LD if eng == "sync" else SP
        P.add(eng, lambda e, o=out, i=in_: e.dma_start(out=o, in_=i), waits, [(s, 16)])
        return (s, P.v(s))

    def vop(emit, waits=()):
        P.add("vector", emit, waits, [(VE, 1)])
        return (VE, P.v(VE))

    def sop(emit, waits=()):
        P.add("scalar", emit, waits, [(SCm, 1)])
        return (SCm, P.v(SCm))

    def top(emit, waits=()):
        P.add("tensor", emit, waits, [(MM, 1)])
        return (MM, P.v(MM))

    def mm_bank(bi, emit, waits=()):
        w = list(waits)
        if bi in bank_free:
            w.append(bank_free[bi])
        return top(emit, w)

    def consume_bank(bi, tok):
        bank_free[bi] = tok

    def rnd_inplace(a, waits=()):
        return vop(lambda e, x=a: e.tensor_scalar(
            out=x, in0=x, scalar1=MAGIC, scalar2=MAGIC, op0=OP.add, op1=OP.subtract),
            waits)

    vop(lambda e: e.memset(cb[:, 0:1], 0.125))
    vop(lambda e: e.memset(cb[:, 1:2], -20.0))
    vop(lambda e: e.memset(cb[:, 2:3], 1e-5))
    vop(lambda e: e.memset(ones[:, :], 1.0))
    vop(lambda e: e.memset(ccs[:, :], 0.0))
    vop(lambda e: e.memset(stt[:, 4:8], 0.0))
    dma("sync", ident[:], id_h)
    dma("sync", triu[:], tr_h)
    consts_v = (LD, P.v(LD))

    def transpose_blocks(src_fn, nblk, sink_fn, pre_waits=(), banks=(6, 7)):
        for bi in range(nblk):
            bk = banks[bi % 2]
            mmv = mm_bank(bk, lambda e, s=src_fn(bi), b2=pb[bk]:
                          e.transpose(b2[:, 0:128], s, ident[:]),
                          tuple(pre_waits) + (consts_v,))
            sink_fn(bi, pb[bk], mmv, bk)

    w0 = ExitStack()
    nat = w0.enter_context(nc.sbuf_tensor("w0nat", [128, 3072], F32))
    tst = [w0.enter_context(nc.sbuf_tensor(f"w0st{i}", [128, 128], F32)) for i in range(2)]
    tst_free = [None, None]
    nat_free = None
    for wname in ("wq", "wk", "wv", "wo"):
        for ot in range(8):
            ldv = dma("sync", nat[:, 0:1024], w_h[wname][128 * ot:128 * (ot + 1), :],
                      waits=[nat_free] if nat_free else ())

            def sink(bi, bank, mmv, bk, wn=wname, o=ot):
                i2 = bi % 2
                w2 = [mmv] + ([tst_free[i2]] if tst_free[i2] else [])
                vv = vop(lambda e, b2=bank, t2=tst[i2]: e.tensor_copy(
                    out=t2[:], in_=b2[:, 0:128]), w2)
                consume_bank(bk, vv)
                tst_free[i2] = dma("gpsimd", wT_h[wn][bi][:, 128 * o:128 * (o + 1)],
                                   tst[i2][:], waits=(vv,))
            transpose_blocks(lambda bi: nat[:, 128 * bi:128 * (bi + 1)], 8, sink,
                             pre_waits=(ldv,))
            nat_free = ("sMM", P.v(MM))
    P.fence()

    dtmp = w0.enter_context(nc.sbuf_tensor("dqtmp", [128, 3072], F32))
    dblk = w0.enter_context(nc.sbuf_tensor("dqblk", [128, 96], F32))
    dbl2 = w0.enter_context(nc.sbuf_tensor("dqbl2", [128, 192], F32))

    def dequant_tile(nat_ap, ncols, ld_tok):
        nb = ncols // 32
        w3 = nat_ap.rearrange("p (a b) -> p a b", b=32)
        vop(lambda e: e.tensor_reduce(out=dblk[:, 0:nb], in_=w3, axis=AX, op=OP.max,
                                      apply_absolute_value=True), (ld_tok,))
        vop(lambda e: e.tensor_scalar(out=dblk[:, 0:nb], in0=dblk[:, 0:nb],
                                      scalar1=1.0 / 7.0, scalar2=1e-12,
                                      op0=OP.mult, op1=OP.max))
        vop(lambda e: e.reciprocal(out=dbl2[:, 0:nb], in_=dblk[:, 0:nb]))
        vop(lambda e: e.tensor_tensor(out=dbl2[:, 96:96 + nb], in0=dblk[:, 0:nb],
                                      in1=dbl2[:, 0:nb], op=OP.mult))
        vop(lambda e: e.tensor_scalar(out=dbl2[:, 96:96 + nb], in0=dbl2[:, 96:96 + nb],
                                      scalar1=-1.0, scalar2=2.0, op0=OP.mult, op1=OP.add))
        vop(lambda e: e.tensor_tensor(out=dbl2[:, 0:nb], in0=dbl2[:, 0:nb],
                                      in1=dbl2[:, 96:96 + nb], op=OP.mult))
        t3 = dtmp[:, 0:ncols].rearrange("p (a b) -> p a b", b=32)
        bc_i = dbl2[:, 0:nb].unsqueeze(2).broadcast_to([128, nb, 32])
        bc_s = dblk[:, 0:nb].unsqueeze(2).broadcast_to([128, nb, 32])
        vop(lambda e: e.tensor_tensor(out=t3, in0=w3, in1=bc_i, op=OP.mult))
        rnd_inplace(dtmp[:, 0:ncols])
        return vop(lambda e: e.tensor_tensor(out=t3, in0=t3, in1=bc_s, op=OP.mult))

    nat_free = None
    for (src_h, kind, notile, ncols) in ((wg_h, "g", 24, D), (wu_h, "u", 24, D),
                                         (wd_h, "d", 8, DF)):
        for ot in range(notile):
            ldv = dma("sync", nat[:, 0:ncols], src_h[128 * ot:128 * (ot + 1), :],
                      waits=[nat_free] if nat_free else ())
            nat_free = dequant_tile(nat[:, 0:ncols], ncols, ldv)

            def sink(bi, bank, mmv, bk, kd=kind, o=ot):
                i2 = bi % 2
                w2 = [mmv] + ([tst_free[i2]] if tst_free[i2] else [])
                vv = vop(lambda e, b2=bank, t2=tst[i2]: e.tensor_copy(
                    out=t2[:], in_=b2[:, 0:128]), w2)
                consume_bank(bk, vv)
                if kd == "g":
                    dst = wgT_h[o][bi][:, :]
                elif kd == "u":
                    dst = wuT_h[o][bi][:, :]
                else:
                    dst = wdT_h[bi][:, 128 * o:128 * (o + 1)]
                tst_free[i2] = dma("gpsimd", dst, tst[i2][:], waits=(vv,))
            transpose_blocks(lambda bi: dtmp[:, 128 * bi:128 * (bi + 1)],
                             ncols // 128, sink, pre_waits=(nat_free,))
    w0.close()
    P.fence()

    ln_state = {"prev": None, "sqrt_prev": None}

    def layernorm(x_ap, sq_ap, ln_ap, ld_tok, track=False, out_free=None):
        rw0 = [ld_tok] + ([ln_state["sqrt_prev"]] if ln_state["sqrt_prev"] else [])
        vop(lambda e: e.reduce_sum(out=stt[:, 0:1], in_=x_ap, axis=AX, negate=True),
            rw0)
        sw = [ld_tok] + ([ln_state["prev"]] if ln_state["prev"] else [])
        s0 = sop(lambda e: e.activation(out=sqscr[:], in_=x_ap, func=AF.Square,
                                        accum_out=stt[:, 1:2]), sw)
        vop(lambda e: e.tensor_scalar(out=stt[:, 2:3], in0=stt[:, 0:1],
                                      scalar1=1.0 / D, scalar2=None, op0=OP.mult), (s0,))
        vop(lambda e: e.tensor_tensor(out=stt[:, 3:4], in0=stt[:, 2:3],
                                      in1=stt[:, 2:3], op=OP.mult))
        vv = vop(lambda e: e.tensor_scalar(out=stt[:, 0:1], in0=stt[:, 1:2],
                                           scalar1=1.0 / D, scalar2=stt[:, 3:4],
                                           op0=OP.mult, op1=OP.subtract))
        ln_state["prev"] = vv
        s1 = sop(lambda e: e.activation(out=stt[:, 6:7], in_=stt[:, 0:1],
                                        func=AF.Sqrt, bias=cb[:, 2:3]), (vv,))
        ln_state["sqrt_prev"] = s1
        vop(lambda e: e.reciprocal(out=stt[:, 7:8], in_=stt[:, 6:7]), (s1,))
        vop(lambda e: e.tensor_scalar(out=stt[:, 3:4], in0=stt[:, 0:1],
                                      scalar1=1e-5, scalar2=None, op0=OP.add))
        vop(lambda e: e.tensor_tensor(out=stt[:, 6:7], in0=stt[:, 7:8],
                                      in1=stt[:, 7:8], op=OP.mult))
        vop(lambda e: e.tensor_tensor(out=stt[:, 6:7], in0=stt[:, 6:7],
                                      in1=stt[:, 3:4], op=OP.mult))
        vop(lambda e: e.tensor_scalar(out=stt[:, 6:7], in0=stt[:, 6:7],
                                      scalar1=-0.5, scalar2=1.5, op0=OP.mult, op1=OP.add))
        vop(lambda e: e.tensor_tensor(out=stt[:, 7:8], in0=stt[:, 7:8],
                                      in1=stt[:, 6:7], op=OP.mult))
        out_v = vop(lambda e: e.tensor_scalar(out=ln_ap, in0=x_ap,
                                              scalar1=stt[:, 2:3], scalar2=stt[:, 7:8],
                                              op0=OP.add, op1=OP.mult),
                    [out_free] if out_free else ())
        if track:
            vop(lambda e: e.tensor_reduce(out=stt[:, 4:5], in_=ln_ap, axis=AX,
                                          op=OP.max, apply_absolute_value=True))
            out_v = vop(lambda e: e.tensor_scalar(out=stt[:, 5:6], in0=stt[:, 4:5],
                                                  scalar1=stt[:, 5:6], scalar2=None,
                                                  op0=OP.max))
        return out_v

    def allreduce_scale(ar_i, col_idx, tgt, last_v):
        d0 = dma("gpsimd", pcol[0:1, 0:128], stt[:, col_idx:col_idx + 1],
                 waits=(last_v,))
        vv = vop(lambda e: e.reduce_max(out=ccs[0:1, 0:1], in_=pcol[0:1, 0:128],
                                        axis=AX), (d0,))
        d1 = dma("gpsimd", ccin[ar_i].ap(), ccs[:], waits=(vv,))
        P.add("gpsimd", lambda e, i=ar_i: e.collective_compute(
            "AllReduce", OP.max, replica_groups=[list(range(NC))],
            ins=[ccin[i].ap().opt()], outs=[ccout[i].ap().opt()]),
            [d1], [(CC, 1)])
        d2 = dma("gpsimd", ccb[:], ccout[ar_i].ap(), waits=((CC, P.v(CC)),))
        vop(lambda e: e.tensor_scalar(out=rw[0:1, 1026:1027], in0=ccb[0:1, 0:1],
                                      scalar1=1.0 / 32767.0, scalar2=1e-12,
                                      op0=OP.mult, op1=OP.max), (d2,))
        vop(lambda e: e.reciprocal(out=rw[0:1, 1027:1028], in_=rw[0:1, 1026:1027]))
        vop(lambda e: e.tensor_tensor(out=rw[0:1, 1028:1029], in0=rw[0:1, 1026:1027],
                                      in1=rw[0:1, 1027:1028], op=OP.mult))
        vop(lambda e: e.tensor_scalar(out=rw[0:1, 1028:1029], in0=rw[0:1, 1028:1029],
                                      scalar1=-1.0, scalar2=2.0, op0=OP.mult, op1=OP.add))
        vv = vop(lambda e: e.tensor_tensor(out=rw[0:1, 1027:1028], in0=rw[0:1, 1027:1028],
                                           in1=rw[0:1, 1028:1029], op=OP.mult))
        mmv = mm_bank(7, lambda e: e.matmul(pb[7][:, 0:2], ones[0:1, 0:128],
                                            rw[0:1, 1026:1028], start=True, stop=True),
                      (vv,))
        vv = vop(lambda e, tg=tgt: e.tensor_copy(out=sc[:, tg:tg + 2], in_=pb[7][:, 0:2]),
                 (mmv,))
        consume_bank(7, vv)
        return vop(lambda e, ci=col_idx: e.memset(stt[:, ci:ci + 1], 0.0))

    katt = ExitStack()
    k_fm = katt.enter_context(nc.sbuf_tensor("k_fm", [128, 8 * S], F32))

    def seq_phase(b):
        pre = ExitStack()
        wqt = pre.enter_context(nc.sbuf_tensor(f"wqt{b}", [128, 8192], F32))
        wkt = pre.enter_context(nc.sbuf_tensor(f"wkt{b}", [128, 8192], F32))
        wvt = pre.enter_context(nc.sbuf_tensor(f"wvt{b}", [128, 8192], F32))
        lng = pre.enter_context(nc.sbuf_tensor(f"lng{b}", [128, 4096], F32))
        xs = [pre.enter_context(nc.sbuf_tensor(f"pxs{b}_{i}", [128, 1024], F32)) for i in range(2)]
        lt = pre.enter_context(nc.sbuf_tensor(f"plt{b}", [128, 1024], F32))
        qst = [pre.enter_context(nc.sbuf_tensor(f"pqst{b}_{i}", [128, 512], F32)) for i in range(2)]
        vxt = pre.enter_context(nc.sbuf_tensor(f"pvxt{b}", [128, 1040], F32))

        wl = {}
        for wn, wt in (("wq", wqt), ("wk", wkt), ("wv", wvt)):
            wl[wn] = dma("sync", wt[:].rearrange("p (k c) -> p k c", k=8),
                         wT_h[wn][:, :, :].transpose([1, 0, 2]))
        xs_free = [None, None]
        lt_state = {}
        qst_free = [None, None]
        vxt_free = None
        for g in range(4):
            t0 = 512 * g
            for t4 in range(4):
                xbuf = xs[t4 % 2]
                lv = dma("sync", xbuf[:], x_h[b][t0 + 128 * t4: t0 + 128 * (t4 + 1), :],
                         waits=[xs_free[t4 % 2]] if xs_free[t4 % 2] else ())
                lnv = layernorm(xbuf[:], None, lt[:], lv,
                                out_free=lt_state.get("tp"))
                xs_free[t4 % 2] = lnv

                def sink(bi, bank, mmv, bk, tl=t4):
                    vv = vop(lambda e, b2=bank, b3=bi, tl2=tl: e.tensor_copy(
                        out=lng[:, 512 * b3 + 128 * tl2: 512 * b3 + 128 * (tl2 + 1)],
                        in_=b2[:, 0:128]), (mmv,))
                    consume_bank(bk, vv)
                transpose_blocks(lambda bi: lt[:, 128 * bi:128 * (bi + 1)], 8, sink,
                                 pre_waits=(lnv,))
                lt_state["tp"] = (MM, P.v(MM))
            ln_ready = (VE, P.v(VE))
            for wn, wt in (("wq", wqt), ("wk", wkt)):
                for oc in range(8):
                    bi = oc % 2
                    for kk in range(8):
                        w2 = (ln_ready, wl[wn]) if kk == 0 else ()
                        mmv = mm_bank(bi, lambda e, b2=pb[bi], w=wt, o=oc, k2=kk:
                                      e.matmul(b2[:, :],
                                               w[:, 1024 * k2 + 128 * o:1024 * k2 + 128 * (o + 1)],
                                               lng[:, 512 * k2:512 * (k2 + 1)],
                                               start=(k2 == 0), stop=(k2 == 7)), w2) \
                            if kk == 0 else top(
                                lambda e, b2=pb[bi], w=wt, o=oc, k2=kk:
                                e.matmul(b2[:, :],
                                         w[:, 1024 * k2 + 128 * o:1024 * k2 + 128 * (o + 1)],
                                         lng[:, 512 * k2:512 * (k2 + 1)],
                                         start=(k2 == 0), stop=(k2 == 7)))
                    if wn == "wq":
                        w3 = [mmv] + ([qst_free[bi]] if qst_free[bi] else [])
                        vv = vop(lambda e, b2=pb[bi], q2=qst[bi]: e.tensor_copy(
                            out=q2[:], in_=b2[:, :]), w3)
                        consume_bank(bi, vv)
                        qst_free[bi] = dma("gpsimd", qT_h[oc][:, t0:t0 + 512],
                                           qst[bi][:], waits=(vv,))
                    else:
                        vv = vop(lambda e, b2=pb[bi], o=oc, t_=t0: e.tensor_copy(
                            out=k_fm[:, 2048 * o + t_: 2048 * o + t_ + 512],
                            in_=b2[:, :]), (mmv,))
                        consume_bank(bi, vv)
            for t4 in range(4):
                tt = 4 * g + t4
                vset = vop(lambda e: e.memset(vxt[:, 64:1040:65], 1.0),
                           [vxt_free] if vxt_free else ())
                for half in range(2):
                    bi = 2 + half
                    for kk in range(8):
                        w2 = (ln_ready, wl["wv"]) if kk == 0 else ()
                        if kk == 0:
                            mmv = mm_bank(bi, lambda e, b2=pb[bi], t42=t4, h=half, k2=kk:
                                          e.matmul(b2[:, :],
                                                   lng[:, 512 * k2 + 128 * t42:512 * k2 + 128 * (t42 + 1)],
                                                   wvt[:, 1024 * k2 + 512 * h:1024 * k2 + 512 * (h + 1)],
                                                   start=True, stop=False), w2)
                        else:
                            mmv = top(lambda e, b2=pb[bi], t42=t4, h=half, k2=kk:
                                      e.matmul(b2[:, :],
                                               lng[:, 512 * k2 + 128 * t42:512 * k2 + 128 * (t42 + 1)],
                                               wvt[:, 1024 * k2 + 512 * h:1024 * k2 + 512 * (h + 1)],
                                               start=False, stop=(k2 == 7)))
                    dst3 = vxt[:, 520 * half:520 * (half + 1)].rearrange(
                        "p (h c) -> p h c", c=65)[:, :, 0:64]
                    src3 = pb[bi][:, :].rearrange("p (h c) -> p h c", c=64)
                    vset = vop(lambda e, d3=dst3, s3=src3: e.tensor_copy(out=d3, in_=s3),
                               (mmv, vset))
                    consume_bank(bi, vset)
                for hp in range(8):
                    vxt_free = dma("gpsimd", v_sc[hp][tt][:, :],
                                   vxt[:, 130 * hp:130 * (hp + 1)], waits=(vset,))
        pre.close()
        P.fence()

        att = ExitStack()
        wot = att.enter_context(nc.sbuf_tensor(f"wot{b}", [128, 8192], F32))
        qsb = att.enter_context(nc.sbuf_tensor(f"aq{b}", [128, 4096], F32))
        vsb = [att.enter_context(nc.sbuf_tensor(f"av{b}_{i}", [128, 2080], F32)) for i in range(2)]
        pA = [att.enter_context(nc.sbuf_tensor(f"apA{b}_{i}", [128, 512], F32)) for i in range(2)]
        pB = [att.enter_context(nc.sbuf_tensor(f"apB{b}_{i}", [128, 512], F32)) for i in range(2)]
        atn = att.enter_context(nc.sbuf_tensor(f"atn{b}", [128, 4096], F32))
        btm = att.enter_context(nc.sbuf_tensor(f"btm{b}", [64, 512], F32))
        axs = att.enter_context(nc.sbuf_tensor(f"axs{b}", [128, 1024], F32))
        bcs = att.enter_context(nc.sbuf_tensor(f"bcs{b}", [64, 1024], F32))
        x1s = [att.enter_context(nc.sbuf_tensor(f"x1s{b}_{i}", [128, 1024], F32)) for i in range(2)]

        wov = dma("sync", wot[:].rearrange("p (k c) -> p k c", k=8),
                  wT_h["wo"][:, :, :].transpose([1, 0, 2]))
        vsb_free = [None, None]
        q_free = None
        x1_free = [None, None]
        norm_done = None
        atn_used = None
        for Q in range(4):
            nch = 4 * Q + 4
            qv = dma("sync", qsb[:].rearrange("p (k c) -> p k c", k=8),
                     qT_h[:, :, 512 * Q:512 * (Q + 1)].transpose([1, 0, 2]),
                     waits=[q_free] if q_free else ())
            shift_last = None
            for hp in range(8):
                vbuf = vsb[hp % 2]
                vld = dma("sync", vbuf[:, 0:130 * nch].rearrange("p (k c) -> p k c", c=130),
                          v_sc[hp, 0:nch].transpose([1, 0, 2]),
                          waits=[vsb_free[hp % 2]] if vsb_free[hp % 2] else ())
                exp_done = {}
                av_done = {}
                pfree = {0: None, 1: None}
                for c in range(nch):
                    bA, bB = c % 2, 2 + c % 2
                    top(lambda e, b2=pb[bA], c2=c, h=hp:
                        e.matmul(b2[:, :],
                                 k_fm[0:64, 2048 * h + 128 * c2:2048 * h + 128 * (c2 + 1)],
                                 qsb[0:64, 512 * h:512 * (h + 1)],
                                 start=True, stop=True, tile_position=(0, 0)),
                        [qv] + ([exp_done[c - 2]] if c >= 2 else [bank_free.get(bA)] if bank_free.get(bA) else []))
                    mmv = top(lambda e, b2=pb[bB], c2=c, h=hp:
                              e.matmul(b2[:, :],
                                       k_fm[64:128, 2048 * h + 128 * c2:2048 * h + 128 * (c2 + 1)],
                                       qsb[64:128, 512 * h:512 * (h + 1)],
                                       start=True, stop=True, tile_position=(64, 0)),
                        [exp_done[c - 2]] if c >= 2 else ([bank_free.get(bB)] if bank_free.get(bB) else []))
                    dd = 128 * (c - 4 * Q) if c >= 4 * Q else -1
                    ew = [mmv]
                    if pfree[c % 2]:
                        ew.append(pfree[c % 2])
                    if dd > 0:
                        mz = vop(lambda e, p2=pA[c % 2], d=dd: e.memset(p2[:, 0:d], 0.0),
                                 [pfree[c % 2]] if pfree[c % 2] else ())
                        mz = vop(lambda e, p2=pB[c % 2], d=dd: e.memset(p2[:, 0:d], 0.0))
                        ew.append(mz)
                    d0 = max(dd, 0)
                    sop(lambda e, b2=pb[bA], p2=pA[c % 2], d=d0:
                        e.activation(out=p2[:, d:512], in_=b2[:, d:512], func=AF.Exp,
                                     scale=cb[:, 0:1], bias=cb[:, 1:2]), ew)
                    ev = sop(lambda e, b2=pb[bB], p2=pB[c % 2], d=d0:
                             e.activation(out=p2[:, d:512], in_=b2[:, d:512], func=AF.Exp,
                                          scale=cb[:, 0:1], bias=cb[:, 1:2]))
                    exp_done[c] = ev
                    avw = [ev, vld]
                    if dd >= 0:
                        mv = vop(lambda e, p2=pA[c % 2], d=d0: e.tensor_tensor(
                            out=p2[:, d:d + 128], in0=p2[:, d:d + 128], in1=triu[:],
                            op=OP.mult), (ev,))
                        mv = vop(lambda e, p2=pB[c % 2], d=d0: e.tensor_tensor(
                            out=p2[:, d:d + 128], in0=p2[:, d:d + 128], in1=triu[:],
                            op=OP.mult))
                        avw.append(mv)
                    if c == 0 and norm_done is not None:
                        avw.append(norm_done)
                    top(lambda e, c2=c, p2=pA[c % 2], v2=vbuf, n=nch:
                        e.matmul(pb[4][0:65, :], v2[:, 130 * c2:130 * c2 + 65], p2[:, :],
                                 start=(c2 == 0), stop=(c2 == n - 1)), avw)
                    av = top(lambda e, c2=c, p2=pB[c % 2], v2=vbuf, n=nch:
                             e.matmul(pb[5][0:65, :], v2[:, 130 * c2 + 65:130 * (c2 + 1)],
                                      p2[:, :], start=(c2 == 0), stop=(c2 == n - 1)))
                    av_done[c] = av
                    pfree[c % 2] = av
                vsb_free[hp % 2] = av_done[nch - 1]
                rv = vop(lambda e: e.reciprocal(out=rw[0:1, 0:512], in_=pb[4][64:65, :]),
                         (av_done[nch - 1],))
                rv = vop(lambda e: e.reciprocal(out=rw[0:1, 512:1024], in_=pb[5][64:65, :]))
                bw = [rv] + ([norm_done] if norm_done is not None else [])
                top(lambda e: e.matmul(pb[6][0:64, :], ones[0:1, 0:64], rw[0:1, 0:512],
                                       start=True, stop=True), bw)
                bcv = top(lambda e: e.matmul(pb[7][0:64, :], ones[0:1, 0:64],
                                             rw[0:1, 512:1024], start=True, stop=True))
                aw = [bcv] + ([atn_used] if atn_used is not None else [])
                vop(lambda e: e.tensor_copy(out=bcs[0:64, 0:512], in_=pb[6][0:64, :]), aw)
                vop(lambda e: e.tensor_copy(out=bcs[0:64, 512:1024], in_=pb[7][0:64, :]))
                vop(lambda e, h=hp: e.tensor_tensor(
                    out=atn[0:64, 512 * h:512 * (h + 1)], in0=pb[4][0:64, :],
                    in1=bcs[0:64, 0:512], op=OP.mult))
                bw2 = [shift_last] if shift_last else []
                norm_done = vop(lambda e: e.tensor_tensor(
                    out=btm[0:64, :], in0=pb[5][0:64, :], in1=bcs[0:64, 512:1024],
                    op=OP.mult), bw2)
                shift_last = dma("gpsimd", atn[64:128, 512 * hp:512 * (hp + 1)],
                                 btm[0:64, :], waits=(norm_done,))
            atn_ready_v = (VE, P.v(VE))
            for t4 in range(4):
                xv = dma("sync", axs[:],
                         x_h[b][512 * Q + 128 * t4: 512 * Q + 128 * (t4 + 1), :],
                         waits=[x1_free[0]] if (t4 > 0 or Q > 0) and x1_free[0] else ())
                evl = None
                for half in range(2):
                    bi = half
                    for oc in range(8):
                        w2 = [shift_last, atn_ready_v] if oc == 0 else []
                        if oc == 0:
                            mmv = mm_bank(bi, lambda e, b2=pb[bi], o=oc, h=half, t42=t4:
                                          e.matmul(b2[:, :],
                                                   atn[:, 512 * o + 128 * t42:512 * o + 128 * (t42 + 1)],
                                                   wot[:, 1024 * o + 512 * h:1024 * o + 512 * (h + 1)],
                                                   start=True, stop=False),
                                          w2 + [wov])
                        else:
                            mmv = top(lambda e, b2=pb[bi], o=oc, h=half, t42=t4:
                                      e.matmul(b2[:, :],
                                               atn[:, 512 * o + 128 * t42:512 * o + 128 * (t42 + 1)],
                                               wot[:, 1024 * o + 512 * h:1024 * o + 512 * (h + 1)],
                                               start=False, stop=(o == 7)))
                    w3 = [mmv, xv] + ([x1_free[1]] if x1_free[1] else [])
                    evl = vop(lambda e, b2=pb[bi], h=half, xb=x1s[t4 % 2]: e.tensor_tensor(
                        out=xb[:, 512 * h:512 * (h + 1)], in0=b2[:, :],
                        in1=axs[:, 512 * h:512 * (h + 1)], op=OP.add), w3)
                    consume_bank(bi, evl)
                atn_used = (MM, P.v(MM))
                r0 = b * S + 512 * Q + 128 * t4
                dtok = dma("gpsimd", x1_h[r0:r0 + 128, :], x1s[t4 % 2][:], waits=(evl,))
                x1_free = [evl, dtok]
            q_free = atn_used
        att.close()
        P.fence()

    for b in range(B):
        seq_phase(b)
    katt.close()

    f0 = ExitStack()
    f0x = [f0.enter_context(nc.sbuf_tensor(f"f0x{i}", [128, 1024], F32)) for i in range(2)]
    f0l = f0.enter_context(nc.sbuf_tensor("f0l", [128, 1024], F32))
    f0t = [f0.enter_context(nc.sbuf_tensor(f"f0t{i}", [128, 128], F32)) for i in range(2)]
    f0x_free = [None, None]
    f0_state = {}
    f0t_free = [None, None]
    amax_v = None
    for tt in range(32):
        xbuf = f0x[tt % 2]
        lv = dma("sync", xbuf[:], x1_h[128 * tt:128 * (tt + 1), :],
                 waits=[f0x_free[tt % 2]] if f0x_free[tt % 2] else ())
        amax_v = layernorm(xbuf[:], None, f0l[:], lv, track=True,
                           out_free=f0_state.get("tp"))
        f0x_free[tt % 2] = amax_v

        def sink0(bi, bank, mmv, bk, t=tt):
            i2 = bi % 2
            w2 = [mmv] + ([f0t_free[i2]] if f0t_free[i2] else [])
            vv = vop(lambda e, b2=bank, t2=f0t[i2]: e.tensor_copy(
                out=t2[:], in_=b2[:, 0:128]), w2)
            consume_bank(bk, vv)
            f0t_free[i2] = dma("gpsimd", ln2_h[bi][:, 128 * t:128 * (t + 1)],
                               f0t[i2][:], waits=(vv,))
        transpose_blocks(lambda bi: f0l[:, 128 * bi:128 * (bi + 1)], 8, sink0,
                         pre_waits=(amax_v,))
        f0_state["tp"] = (MM, P.v(MM))
    f0.close()
    P.fence()
    allreduce_scale(0, 5, 0, amax_v)
    P.fence()

    f1 = ExitStack()
    lnt = [f1.enter_context(nc.sbuf_tensor(f"f1l{i}", [128, 8192], F32)) for i in range(2)]
    wst = [f1.enter_context(nc.sbuf_tensor(f"f1w{i}", [128, 1024], F32)) for i in range(2)]
    gst = [f1.enter_context(nc.sbuf_tensor(f"f1g{i}", [128, 1024], F32)) for i in range(2)]
    lnt_free = [None, None]
    wst_free = [None, None]
    gst_free = [None, None]
    amax_g_v = None
    for tq in range(4):
        t0 = 1024 * tq
        lbuf = lnt[tq % 2]
        lv = dma("sync", lbuf[:].rearrange("p (k c) -> p k c", k=8),
                 ln2_h[:, :, t0:t0 + 1024].transpose([1, 0, 2]),
                 waits=[lnt_free[tq % 2]] if lnt_free[tq % 2] else ())
        vop(lambda e, l=lbuf: e.tensor_scalar(out=l[:], in0=l[:], scalar1=sc[:, 1:2],
                                              scalar2=None, op0=OP.mult), (lv,))
        qr_v = rnd_inplace(lbuf[:])
        lnt_free[tq % 2] = qr_v
        for wi, (wth, sp_h, acol) in enumerate(((wgT_h, graw_h, 5), (wuT_h, uraw_h, 6))):
            for oc in range(24):
                wbuf = wst[oc % 2]
                wv = dma("sync", wbuf[:].rearrange("p (k c) -> p k c", k=8),
                         wth[oc].transpose([1, 0, 2]),
                         waits=[wst_free[oc % 2]] if wst_free[oc % 2] else ())
                sbuf_g = gst[oc % 2]
                evt = None
                for th in range(2):
                    bi = (oc % 2) * 2 + th
                    for kk in range(8):
                        w2 = [wv, qr_v] if kk == 0 else []
                        if kk == 0:
                            mmv = mm_bank(bi, lambda e, b2=pb[bi], w=wbuf, k2=kk, h=th, l=lbuf:
                                          e.matmul(b2[:, :], w[:, 128 * k2:128 * (k2 + 1)],
                                                   l[:, 1024 * k2 + 512 * h:1024 * k2 + 512 * (h + 1)],
                                                   start=True, stop=False), w2)
                        else:
                            mmv = top(lambda e, b2=pb[bi], w=wbuf, k2=kk, h=th, l=lbuf:
                                      e.matmul(b2[:, :], w[:, 128 * k2:128 * (k2 + 1)],
                                               l[:, 1024 * k2 + 512 * h:1024 * k2 + 512 * (h + 1)],
                                               start=False, stop=(k2 == 7)))
                    w3 = [mmv] + ([gst_free[oc % 2]] if th == 0 and gst_free[oc % 2] else [])
                    vv = vop(lambda e, b2=pb[bi], g2=sbuf_g, h=th: e.tensor_scalar(
                        out=g2[:, 512 * h:512 * (h + 1)], in0=b2[:, :],
                        scalar1=sc[:, 0:1], scalar2=None, op0=OP.mult), w3)
                    consume_bank(bi, vv)
                    vop(lambda e, g2=sbuf_g, h=th: e.tensor_reduce(
                        out=stt[:, 7:8], in_=g2[:, 512 * h:512 * (h + 1)], axis=AX,
                        op=OP.max, apply_absolute_value=True))
                    evt = vop(lambda e, a=acol: e.tensor_scalar(
                        out=stt[:, a:a + 1], in0=stt[:, 7:8],
                        scalar1=stt[:, a:a + 1], scalar2=None, op0=OP.max))
                wst_free[oc % 2] = mmv
                gst_free[oc % 2] = dma("gpsimd",
                                       sp_h[128 * oc:128 * (oc + 1), t0:t0 + 1024],
                                       sbuf_g[:], waits=(evt,))
                amax_g_v = evt
        lnt_free[tq % 2] = mmv
    f1.close()
    P.fence()
    allreduce_scale(1, 5, 2, amax_g_v)
    allreduce_scale(2, 6, 4, amax_g_v)
    P.fence()

    f2 = ExitStack()
    ga = [f2.enter_context(nc.sbuf_tensor(f"f2a{i}", [128, 4096], F32)) for i in range(2)]
    gb = f2.enter_context(nc.sbuf_tensor("f2gb", [128, 4096], F32))
    gc = f2.enter_context(nc.sbuf_tensor("f2gc", [128, 4096], F32))
    gd = f2.enter_context(nc.sbuf_tensor("f2gd", [128, 4096], F32))
    i16a = f2.enter_context(nc.sbuf_tensor("f2ia", [128, 4096], I16))
    u16a = f2.enter_context(nc.sbuf_tensor("f2ua", [128, 4096], U16))
    i16b = f2.enter_context(nc.sbuf_tensor("f2ib", [128, 4096], I16))
    ga_free = [None, None]
    ia_free = None
    ua_free = None
    ib_free = None
    t_amax_v = None
    sv_prev = None
    for oc in range(24):
        buf = ga[oc % 2]
        lv = dma("sync", buf[:], graw_h[128 * oc:128 * (oc + 1), :],
                 waits=[ga_free[oc % 2]] if ga_free[oc % 2] else ())
        vop(lambda e, a=buf: e.tensor_scalar(out=a[:], in0=a[:], scalar1=sc[:, 3:4],
                                             scalar2=None, op0=OP.mult), (lv,))
        rnd_inplace(buf[:])
        iv = vop(lambda e, a=buf: e.tensor_copy(out=i16a[:], in_=a[:]),
                 [ia_free] if ia_free else ())
        ia_free = dma("gpsimd", gi_h[128 * oc:128 * (oc + 1), :], i16a[:], waits=(iv,))
        gv = vop(lambda e, a=buf: e.tensor_scalar(out=gb[:], in0=a[:], scalar1=sc[:, 2:3],
                                                  scalar2=None, op0=OP.mult),
                 [sv_prev] if sv_prev else ())
        ga_free[oc % 2] = gv
        sv = sop(lambda e: e.activation(out=gc[:], in_=gb[:], func=AF.Sigmoid), (gv,))
        sv_prev = sv
        vop(lambda e: e.tensor_scalar(out=gc[:], in0=gc[:], scalar1=65536.0,
                                      scalar2=None, op0=OP.mult), (sv,))
        rnd_inplace(gc[:])
        vop(lambda e: e.tensor_scalar(out=gc[:], in0=gc[:], scalar1=65535.0,
                                      scalar2=None, op0=OP.min))
        iv = vop(lambda e: e.tensor_copy(out=u16a[:], in_=gc[:]),
                 [ua_free] if ua_free else ())
        ua_free = dma("gpsimd", mi_h[128 * oc:128 * (oc + 1), :], u16a[:], waits=(iv,))
        vop(lambda e: e.scalar_tensor_tensor(out=gc[:], in0=gc[:], scalar=1.0 / 65536.0,
                                             in1=gb[:], op0=OP.mult, op1=OP.mult))
        vop(lambda e: e.tensor_reduce(out=stt[:, 7:8], in_=gc[:], axis=AX, op=OP.max,
                                      apply_absolute_value=True))
        t_amax_v = vop(lambda e: e.tensor_scalar(out=stt[:, 5:6], in0=stt[:, 7:8],
                                                 scalar1=stt[:, 5:6], scalar2=None,
                                                 op0=OP.max))
    P.fence()
    allreduce_scale(3, 5, 6, t_amax_v)
    P.fence()

    o_amax_v = None
    ga_free = [None, None]
    for oc in range(24):
        buf = ga[oc % 2]
        lv = dma("sync", buf[:], uraw_h[128 * oc:128 * (oc + 1), :],
                 waits=[ga_free[oc % 2]] if ga_free[oc % 2] else ())
        l2 = dma("sync", i16a[:], gi_h[128 * oc:128 * (oc + 1), :],
                 waits=[ia_free] if ia_free else ())
        l3 = dma("sync", u16a[:], mi_h[128 * oc:128 * (oc + 1), :],
                 waits=[ua_free] if ua_free else ())
        vop(lambda e, a=buf: e.tensor_scalar(out=a[:], in0=a[:], scalar1=sc[:, 5:6],
                                             scalar2=None, op0=OP.mult), (lv,))
        rnd_inplace(buf[:])
        iv = vop(lambda e, a=buf: e.tensor_copy(out=i16b[:], in_=a[:]),
                 [ib_free] if ib_free else ())
        ib_free = dma("gpsimd", ui_h[128 * oc:128 * (oc + 1), :], i16b[:], waits=(iv,))
        vop(lambda e: e.tensor_copy(out=gb[:], in_=i16a[:]), (l2,))
        ia_free = vop(lambda e: e.tensor_scalar(out=gb[:], in0=gb[:], scalar1=sc[:, 2:3],
                                                scalar2=None, op0=OP.mult))
        vop(lambda e: e.tensor_copy(out=gc[:], in_=u16a[:]), (l3,))
        ua_free = vop(lambda e: e.scalar_tensor_tensor(
            out=gc[:], in0=gc[:], scalar=1.0 / 65536.0, in1=gb[:],
            op0=OP.mult, op1=OP.mult))
        vop(lambda e: e.tensor_scalar(out=gd[:], in0=gc[:], scalar1=sc[:, 7:8],
                                      scalar2=None, op0=OP.mult))
        rnd_inplace(gd[:])
        iv = vop(lambda e: e.tensor_copy(out=i16a[:], in_=gd[:]))
        ia_free = dma("gpsimd", gi2_h[128 * oc:128 * (oc + 1), :], i16a[:], waits=(iv,))
        vop(lambda e: e.tensor_scalar(out=gd[:], in0=gd[:], scalar1=sc[:, 6:7],
                                      scalar2=None, op0=OP.mult))
        ga_free[oc % 2] = vop(lambda e, a=buf: e.scalar_tensor_tensor(
            out=gc[:], in0=a[:], scalar=sc[:, 4:5], in1=gd[:], op0=OP.mult, op1=OP.mult))
        vop(lambda e: e.tensor_reduce(out=stt[:, 7:8], in_=gc[:], axis=AX, op=OP.max,
                                      apply_absolute_value=True))
        o_amax_v = vop(lambda e: e.tensor_scalar(out=stt[:, 6:7], in0=stt[:, 7:8],
                                                 scalar1=stt[:, 6:7], scalar2=None,
                                                 op0=OP.max))
    f2.close()
    P.fence()
    allreduce_scale(4, 6, 8, o_amax_v)
    P.fence()

    f2c = ExitStack()
    wdt = f2c.enter_context(nc.sbuf_tensor("wdt", [128, 24576], F32))
    oi = f2c.enter_context(nc.sbuf_tensor("oi", [128, 6144], F32))
    ci16 = [f2c.enter_context(nc.sbuf_tensor(f"c16{i}", [128, 6144], I16)) for i in range(2)]
    cu16 = [f2c.enter_context(nc.sbuf_tensor(f"cu16{i}", [128, 6144], I16)) for i in range(2)]
    csc = f2c.enter_context(nc.sbuf_tensor("csc", [128, 512], F32))
    cx1 = f2c.enter_context(nc.sbuf_tensor("cx1", [128, 1024], F32))
    cys = [f2c.enter_context(nc.sbuf_tensor(f"cys{i}", [128, 1024], F32)) for i in range(2)]
    wdl = dma("sync", wdt[:].rearrange("p (k c) -> p k c", k=24),
              wdT_h[:, :, :].transpose([1, 0, 2]))
    c16_free = [None, None]
    cy_free = [None, None]
    oi_free = None
    x1v = None
    for tq in range(16):
        t0 = 256 * tq
        ib2 = tq % 2
        lg = [dma("sync", ci16[ib2][:, 256 * ic:256 * (ic + 1)],
                  gi2_h[128 * ic:128 * (ic + 1), t0:t0 + 256],
                  waits=[c16_free[ib2]] if ic == 0 and c16_free[ib2] else ())
              for ic in range(24)]
        lu = [dma("sync", cu16[ib2][:, 256 * ic:256 * (ic + 1)],
                  ui_h[128 * ic:128 * (ic + 1), t0:t0 + 256])
              for ic in range(24)]
        oiv = None
        for ic in range(24):
            w2 = [lg[ic], lu[ic]] + ([oi_free] if ic == 0 and oi_free else [])
            vop(lambda e, i2=ib2, c2=ic: e.tensor_copy(
                out=csc[:, 0:256], in_=ci16[i2][:, 256 * c2:256 * (c2 + 1)]), w2)
            vop(lambda e: e.tensor_scalar(out=csc[:, 0:256], in0=csc[:, 0:256],
                                          scalar1=sc[:, 6:7], scalar2=None, op0=OP.mult))
            vop(lambda e, i2=ib2, c2=ic: e.tensor_copy(
                out=csc[:, 256:512], in_=cu16[i2][:, 256 * c2:256 * (c2 + 1)]))
            vop(lambda e, c2=ic: e.scalar_tensor_tensor(
                out=oi[:, 256 * c2:256 * (c2 + 1)], in0=csc[:, 256:512],
                scalar=sc[:, 4:5], in1=csc[:, 0:256], op0=OP.mult, op1=OP.mult))
            vop(lambda e, c2=ic: e.tensor_scalar(
                out=oi[:, 256 * c2:256 * (c2 + 1)], in0=oi[:, 256 * c2:256 * (c2 + 1)],
                scalar1=sc[:, 9:10], scalar2=None, op0=OP.mult))
            oiv = rnd_inplace(oi[:, 256 * ic:256 * (ic + 1)])
        c16_free[ib2] = oiv
        last_mm = None
        for m2 in range(2):
            r0 = t0 + 128 * m2
            xv = dma("sync", cx1[:], x1_h[r0:r0 + 128, :],
                     waits=[x1v] if x1v else ())
            ybuf = cys[m2]
            evl = None
            for half in range(2):
                bi = m2 * 2 + half
                for ic in range(24):
                    w2 = [oiv, wdl] if ic == 0 else []
                    if ic == 0:
                        mmv = mm_bank(bi, lambda e, b2=pb[bi], c2=ic, m=m2, h=half:
                                      e.matmul(b2[:, :],
                                               oi[:, 256 * c2 + 128 * m:256 * c2 + 128 * (m + 1)],
                                               wdt[:, 1024 * c2 + 512 * h:1024 * c2 + 512 * (h + 1)],
                                               start=True, stop=False), w2)
                    else:
                        mmv = top(lambda e, b2=pb[bi], c2=ic, m=m2, h=half:
                                  e.matmul(b2[:, :],
                                           oi[:, 256 * c2 + 128 * m:256 * c2 + 128 * (m + 1)],
                                           wdt[:, 1024 * c2 + 512 * h:1024 * c2 + 512 * (h + 1)],
                                           start=False, stop=(c2 == 23)))
                w3 = [mmv, xv] + ([cy_free[m2]] if half == 0 and cy_free[m2] else [])
                evl = vop(lambda e, b2=pb[bi], h=half, yb=ybuf: e.scalar_tensor_tensor(
                    out=yb[:, 512 * h:512 * (h + 1)], in0=b2[:, :], scalar=sc[:, 8:9],
                    in1=cx1[:, 512 * h:512 * (h + 1)], op0=OP.mult, op1=OP.add), w3)
                consume_bank(bi, evl)
            x1v = evl
            last_mm = mmv
            b_ = r0 // S
            s0_ = r0 % S
            cy_free[m2] = dma("gpsimd", y_h[b_][s0_:s0_ + 128, :], ybuf[:], waits=(evl,))
        oi_free = last_mm
    f2c.close()
    P.add("gpsimd", None, [(SP, P.v(SP)), (LD, P.v(LD)), (MM, P.v(MM)),
                           (VE, P.v(VE)), (SCm, P.v(SCm))], [])

    with nc.Block() as blk:
        P.emit(blk)
    st.close()
    return nc


_NC_CACHE = {}


def kernel(L0_attn_x, L0_attn_Wq, L0_attn_Wk, L0_attn_Wv, L0_attn_Wo,
           L0_ffn_w_gate, L0_ffn_w_up, L0_ffn_w_down):
    x = np.ascontiguousarray(np.asarray(L0_attn_x, dtype=np.float32))
    ws = {n: np.ascontiguousarray(np.asarray(a, dtype=np.float32))
          for n, a in (("wq", L0_attn_Wq), ("wk", L0_attn_Wk), ("wv", L0_attn_Wv),
                       ("wo", L0_attn_Wo), ("wg", L0_ffn_w_gate),
                       ("wu", L0_ffn_w_up), ("wd", L0_ffn_w_down))}
    ident = np.eye(128, dtype=np.float32)
    triu = np.triu(np.ones((128, 128), dtype=np.float32))
    if "nc" not in _NC_CACHE:
        _NC_CACHE["nc"] = build_nc()
    nc = _NC_CACHE["nc"]
    in_maps = []
    for c in range(NC):
        m = {"x": x[2 * c:2 * c + 2], "ident": ident, "triu": triu}
        m.update(ws)
        in_maps.append(m)
    res = run_bass_kernel_spmd(nc, in_maps, core_ids=list(range(NC)))
    return np.concatenate([np.asarray(r["y"]) for r in res.results], axis=0)


if __name__ == "__main__":
    import sys
    np.random.seed(0)
    x = np.random.randn(16, 2048, 1024).astype(np.float32)
    print("build only test")
    nc = build_nc()
    print("built ok")
